# version 03
# baseline (speedup 1.0000x reference)
import numpy as np
import ml_dtypes
import concourse.bass as bass
import concourse.mybir as mybir
from concourse.bass_utils import run_bass_kernel_spmd

F32 = mybir.dt.float32
BF16 = mybir.dt.bfloat16
AF = mybir.ActivationFunctionType
ALU = mybir.AluOpType
AX = mybir.AxisListType

D = 1024
S = 2048
NT = 16
NC = 8
NE = 8
FF = 2048
NFT = 16
NH = 16
SCALE = 0.125
NEGBIG = -1.0e6
EPS = 1e-5

ENGS = ("tensor", "vector", "scalar", "sync", "gpsimd")
ALL_STAGES = ("ffn1", "attn", "ffn2")


class Prog:

    def __init__(self):
        self.streams = {e: [] for e in ENGS}
        self.cnt = {s: 0 for s in ("ld", "w", "pe", "v", "sc", "st")}

    def op(self, eng, fn, waits=(), inc=None, n=1, nospace=False):
        ws = [w for w in waits if w is not None]
        self.streams[eng].append((fn, ws, inc, n, nospace))
        if inc is not None:
            self.cnt[inc] += n
            return (inc, self.cnt[inc])
        return None

    def mark(self, sem):
        return (sem, self.cnt[sem])


def build_program(stages=ALL_STAGES, debug_dump=False, ATTN_SIMPLE=False, ATTN_LEVEL=9, PIPE_ATTN=True):
    nc = bass.Bass("TRN2", target_bir_lowering=False, detect_race_conditions=False)

    x_d = nc.dram_tensor("x", [S, D], F32, kind="ExternalInput")
    wgT_d = nc.dram_tensor("wgT", [NE, D, FF], F32, kind="ExternalInput")
    wuT_d = nc.dram_tensor("wuT", [NE, D, FF], F32, kind="ExternalInput")
    wdT_d = nc.dram_tensor("wdT", [NE, FF, D], F32, kind="ExternalInput")
    wqT_d = nc.dram_tensor("wqT", [D, D], F32, kind="ExternalInput")
    wkT_d = nc.dram_tensor("wkT", [D, D], F32, kind="ExternalInput")
    wvT_d = nc.dram_tensor("wvT", [D, D], F32, kind="ExternalInput")
    woT_d = nc.dram_tensor("woT", [D, D], F32, kind="ExternalInput")
    rT_d = nc.dram_tensor("rT", [128, 64], F32, kind="ExternalInput")
    bpack_d = nc.dram_tensor("bpack", [128, 32], F32, kind="ExternalInput")
    identb_d = nc.dram_tensor("identb", [128, 128], BF16, kind="ExternalInput")
    identf_d = nc.dram_tensor("identf", [128, 128], F32, kind="ExternalInput")
    maskwA_d = nc.dram_tensor("maskwA", [128, 512], BF16, kind="ExternalInput")
    maskwB_d = nc.dram_tensor("maskwB", [128, 256], BF16, kind="ExternalInput")
    onecol_d = nc.dram_tensor("onecol", [128, 1], BF16, kind="ExternalInput")
    onerow_d = nc.dram_tensor("onerow", [1, 128], BF16, kind="ExternalInput")
    out_d = nc.dram_tensor("out", [S, D], F32, kind="ExternalOutput")

    x_sb = nc.alloc_sbuf_tensor("x_sb", [128, NT * D], F32)
    xn = nc.alloc_sbuf_tensor("xn", [128, NC * S], BF16)
    regA = nc.alloc_sbuf_tensor("regA", [128, 65536], mybir.dt.uint8)
    a0 = nc.lookup_mloc(regA).addr
    h_sb = nc.alloc_sbuf_tensor_at("h_sb", [128, NFT * 1024], BF16, offset=a0)
    wd_sb = nc.alloc_sbuf_tensor_at("wd_sb", [128, 16 * D], BF16, offset=a0 + 32768)
    wq_sb = nc.alloc_sbuf_tensor_at("wq_sb", [128, NC * D], BF16, offset=a0)
    wk_sb = nc.alloc_sbuf_tensor_at("wk_sb", [128, NC * D], BF16, offset=a0 + 16384)
    wv_sb = nc.alloc_sbuf_tensor_at("wv_sb", [128, NC * D], BF16, offset=a0 + 32768)
    wo_sb = nc.alloc_sbuf_tensor_at("wo_sb", [128, NC * D], BF16, offset=a0 + 49152)
    regB = nc.alloc_sbuf_tensor("regB", [128, 40960], mybir.dt.uint8)
    b0 = nc.lookup_mloc(regB).addr
    wg_buf = [nc.alloc_sbuf_tensor_at(f"wg{p}", [128, NC * 512], BF16,
                                      offset=b0 + p * 8192) for p in range(2)]
    wu_buf = [nc.alloc_sbuf_tensor_at(f"wu{p}", [128, NC * 512], BF16,
                                      offset=b0 + 16384 + p * 8192) for p in range(2)]
    silu_sb = nc.alloc_sbuf_tensor_at("silu_sb", [128, 2 * 1024], BF16, offset=b0 + 32768)
    qf = nc.alloc_sbuf_tensor_at("qf", [128, 8 * 512], BF16, offset=b0)
    kf = nc.alloc_sbuf_tensor_at("kf", [128, 5 * 1024], BF16, offset=b0 + 8192)
    vt = nc.alloc_sbuf_tensor_at("vt", [128, 5 * 1024], BF16, offset=b0 + 18432)
    E_sb = nc.alloc_sbuf_tensor_at("E_sb", [128, 2048], BF16, offset=b0 + 28672)
    ctm = nc.alloc_sbuf_tensor_at("ctm", [128, 1024], BF16, offset=b0 + 32768)
    cfm = nc.alloc_sbuf_tensor_at("cfm", [128, 1024], BF16, offset=b0 + 34816)
    bvb = nc.alloc_sbuf_tensor_at("bvb", [128, 1024], F32, offset=b0 + 36864)
    identb = nc.alloc_sbuf_tensor("identb_sb", [128, 128], BF16)
    identf = nc.alloc_sbuf_tensor("identf_sb", [128, 128], F32)
    maskwA = nc.alloc_sbuf_tensor("maskwA_sb", [128, 512], BF16)
    maskwB = nc.alloc_sbuf_tensor("maskwB_sb", [128, 256], BF16)
    bpack = nc.alloc_sbuf_tensor("bpack_sb", [128, 32], F32)
    rT = nc.alloc_sbuf_tensor("rT_sb", [128, 64], BF16)
    onecol = nc.alloc_sbuf_tensor("onecol_sb", [128, 1], BF16)
    onerow = nc.alloc_sbuf_tensor("onerow_sb", [1, 128], BF16)
    stats = nc.alloc_sbuf_tensor("stats", [128, 128], F32)
    Lg = nc.alloc_sbuf_tensor("Lg", [128, 128], F32)
    dwh = nc.alloc_sbuf_tensor("dwh", [128, 128], F32)
    rsc = nc.alloc_sbuf_tensor("rsc", [128, 160], F32)
    zsb = nc.alloc_sbuf_tensor("zsb", [128, 48], F32)
    spacer = nc.alloc_sbuf_tensor("spacer", [1, 4], F32)
    stmp = nc.alloc_sbuf_tensor("stmp", [128, 512], F32)

    ST_MS, ST_QS, ST_M, ST_V, ST_R = 0, 16, 32, 48, 64

    ps = [nc.alloc_psum_tensor(f"ps{i}", [128, 1024], F32) for i in range(4)]
    psb = [p.bitcast(BF16) for p in ps]

    pr = Prog()

    bank_ready = [None] * 8
    bank_free = [None] * 8
    wbuf_free = {}
    xn_tile_ready = [None] * NT
    st = {"xn_pe": None, "x_w": None, "regA_pe": None, "regB_pe": None,
          "regB_sc": None, "ln_dump": None, "wd_lastrd": None,
          "silu_free": [None, None]}

    def pair(pi):
        return (2 * pi, 2 * pi + 1)

    def wb(bks, which):
        arr = bank_ready if which == "r" else bank_free
        return [arr[b] for b in bks if arr[b] is not None]

    def ph_load():
        def f_x(sy):
            return sy.dma_start(
                x_sb[:].rearrange("p (t h) -> p t h", t=NT),
                x_d[:, :].rearrange("(t p) h -> p t h", p=128))
        pr.op("sync", f_x, inc="ld", n=16)
        for (dst, src) in ((identb, identb_d), (identf, identf_d),
                           (maskwA, maskwA_d), (maskwB, maskwB_d),
                           (bpack, bpack_d), (onecol, onecol_d), (onerow, onerow_d)):
            pr.op("sync", (lambda sy, d=dst, s=src: sy.dma_start(d[:], s[:, :])),
                  inc="ld", n=16)
        pr.op("gpsimd", lambda g: g.dma_start(rT[:], rT_d[:, :]), inc="w", n=16)

    LD_ALL = ("ld", 16 * 8)
    W_RT = ("w", 16)

    def ph_ln(with_route=False):
        for t in range(NT):
            xt = x_sb[:, t * D:(t + 1) * D]
            pr.op("vector", (lambda v, xt=xt, t=t: v.reduce_sum(
                stats[:, ST_MS + t:ST_MS + t + 1], xt, axis=AX.X)),
                waits=[LD_ALL, st["x_w"]])
            pr.op("scalar", (lambda s, xt=xt, t=t: s.activation(
                wd_sb[:, t * 1024:(t + 1) * 1024], xt, AF.Square,
                accum_out=stats[:, ST_QS + t:ST_QS + t + 1])),
                waits=[LD_ALL, st["x_w"], st["wd_lastrd"]], inc="sc")
        st["ln_dump"] = pr.mark("sc")
        sq_done = pr.mark("sc")
        pr.op("vector", lambda v: v.tensor_scalar(
            stats[:, ST_M:ST_M + 16], stats[:, ST_MS:ST_MS + 16], 1.0 / D, None, ALU.mult))
        pr.op("vector", lambda v: v.tensor_scalar(
            stats[:, ST_V:ST_V + 16], stats[:, ST_QS:ST_QS + 16], 1.0 / D, None, ALU.mult),
            waits=[sq_done])
        pr.op("vector", lambda v: v.tensor_tensor(
            rsc[:, 0:16], stats[:, ST_M:ST_M + 16], stats[:, ST_M:ST_M + 16], ALU.mult))
        pr.op("vector", lambda v: v.tensor_tensor(
            stats[:, ST_V:ST_V + 16], stats[:, ST_V:ST_V + 16], rsc[:, 0:16],
            ALU.subtract))
        vvar = pr.op("vector", lambda v: v.tensor_scalar(
            stats[:, ST_V:ST_V + 16], stats[:, ST_V:ST_V + 16], EPS, None,
            ALU.add), inc="v")
        vstd = pr.op("scalar", lambda s: s.activation(
            rsc[:, 16:32], stats[:, ST_V:ST_V + 16], AF.Sqrt),
            waits=[vvar], inc="sc")
        vrstd = pr.op("vector", lambda v: v.reciprocal(
            stats[:, ST_R:ST_R + 16], rsc[:, 16:32]), waits=[vstd], inc="v")
        for t in range(NT):
            stage = h_sb[:, t * 1024:(t + 1) * 1024]
            xt = x_sb[:, t * D:(t + 1) * D]
            vst = pr.op("vector", (lambda v, stage=stage, xt=xt, t=t: v.tensor_scalar(
                stage, xt, stats[:, ST_M + t:ST_M + t + 1],
                stats[:, ST_R + t:ST_R + t + 1], ALU.subtract, ALU.mult)),
                waits=[vrstd, st["regA_pe"]], inc="v")
            pk = 6 + (t % 2)
            half = (t % 2) * 1024
            def f_tr(tn, stage=stage, half=half):
                ins = None
                for c in range(NC):
                    ins = tn.transpose(psb[3][:, half + c * 128:half + (c + 1) * 128],
                                       stage[:, c * 128:(c + 1) * 128], identb[:])
                return ins
            tr = pr.op("tensor", f_tr, waits=[vst] + wb([pk], "f"), inc="pe")
            bank_ready[pk] = tr
            def f_cp(v, t=t, half=half):
                dst = bass.AP(xn, t * 128, [[NC * S, 128], [S, NC], [1, 128]])
                return v.tensor_copy(dst, psb[3][:, half:half + 1024]
                                     .rearrange("p (c f) -> p c f", c=NC))
            cp = pr.op("vector", f_cp, waits=[tr, st["xn_pe"]], inc="v")
            bank_free[pk] = cp
            xn_tile_ready[t] = cp
            if with_route:
                ph_route_logits_tile(t)

    def ph_route_logits_tile(t):
        bk = 4
        def f_mm(tn, t=t):
            ins = None
            for c in range(NC):
                ins = tn.matmul(ps[2][:, t * 8:(t + 1) * 8],
                                xn[:, c * S + t * 128: c * S + (t + 1) * 128],
                                rT[:, c * 8:(c + 1) * 8],
                                start=(t == 0 and c == 0),
                                stop=(t == NT - 1 and c == NC - 1),
                                skip_group_check=True)
            return ins
        wt = [xn_tile_ready[t], W_RT]
        if t == 0:
            wt += wb([bk], "f")
        mm = pr.op("tensor", f_mm, waits=wt, inc="pe")
        if t == NT - 1:
            bank_ready[bk] = mm

    def ph_route():
        bk = 4
        mm = bank_ready[bk]
        st["xn_pe"] = pr.mark("pe")
        cp = pr.op("vector", lambda v: v.tensor_copy(Lg[:], ps[2][:, 0:128]),
                   waits=[mm], inc="v")
        bank_free[bk] = cp

        def c_(a, b):
            return rsc[:, a:b]
        for t in range(NT):
            L = Lg[:, t * 8:(t + 1) * 8]
            pr.op("vector", (lambda v, L=L: v.reduce_max(zsb[:, 32:33], L, axis=AX.X)),
                  waits=[cp])
            pr.op("vector", (lambda v, L=L: v.tensor_scalar(
                c_(32, 40), L, zsb[:, 32:33], None, ALU.subtract)))
            pr.op("vector", (lambda v, L=L: v.tensor_scalar(
                c_(40, 48), L, -1.0, None, ALU.mult)))
            pr.op("vector", (lambda v, L=L: v.tensor_tensor(
                c_(48, 56), L, c_(40, 48), ALU.max)))
            pr.op("vector", lambda v: v.tensor_scalar(
                c_(56, 64), c_(48, 56), zsb[:, 32:33], -0.02, ALU.max, ALU.mult))
            pr.op("vector", lambda v: v.tensor_tensor(
                c_(72, 80), c_(32, 40), c_(56, 64), ALU.is_ge))
            v1 = pr.op("vector", (lambda v, L=L: v.tensor_scalar(
                c_(80, 88), L, zsb[:, 32:33], None, ALU.is_equal)), inc="v")
            se = pr.op("scalar", lambda s: s.activation(
                c_(120, 128), c_(32, 40), AF.Exp), waits=[v1], inc="sc")
            pr.op("vector", lambda v: v.tensor_tensor(
                c_(128, 136), c_(120, 128), c_(72, 80), ALU.mult), waits=[se])
            pr.op("vector", lambda v: v.reduce_sum(zsb[:, 33:34], c_(128, 136), axis=AX.X))
            pr.op("vector", lambda v: v.reciprocal(zsb[:, 34:35], zsb[:, 33:34]))
            pr.op("vector", lambda v: v.tensor_scalar(
                zsb[:, 35:36], zsb[:, 34:35], 0.5, None, ALU.mult))
            pr.op("vector", (lambda v, L=L: v.scalar_tensor_tensor(
                c_(88, 96), c_(80, 88), -1.0e30, L, ALU.mult, ALU.add)))
            pr.op("vector", lambda v: v.reduce_max(zsb[:, 36:37], c_(88, 96), axis=AX.X))
            pr.op("vector", lambda v: v.tensor_scalar(
                c_(96, 104), c_(88, 96), zsb[:, 36:37], None, ALU.subtract))
            pr.op("vector", (lambda v, L=L: v.tensor_scalar(
                c_(104, 112), L, zsb[:, 36:37], None, ALU.subtract)))
            pr.op("vector", lambda v: v.tensor_scalar(
                c_(56, 64), c_(48, 56), zsb[:, 36:37], -0.02, ALU.max, ALU.mult))
            pr.op("vector", lambda v: v.tensor_tensor(
                c_(72, 80), c_(104, 112), c_(56, 64), ALU.is_ge))
            v2 = pr.op("vector", lambda v: v.tensor_scalar(
                c_(112, 120), c_(88, 96), zsb[:, 36:37], None, ALU.is_equal), inc="v")
            se2 = pr.op("scalar", lambda s: s.activation(
                c_(120, 128), c_(96, 104), AF.Exp), waits=[v2], inc="sc")
            pr.op("vector", lambda v: v.tensor_tensor(
                c_(128, 136), c_(120, 128), c_(72, 80), ALU.mult), waits=[se2])
            pr.op("vector", lambda v: v.reduce_sum(zsb[:, 37:38], c_(128, 136), axis=AX.X))
            pr.op("vector", lambda v: v.reciprocal(zsb[:, 38:39], zsb[:, 37:38]))
            pr.op("vector", lambda v: v.tensor_scalar(
                zsb[:, 39:40], zsb[:, 38:39], 0.5, None, ALU.mult))
            pr.op("vector", lambda v: v.tensor_scalar(
                c_(136, 144), c_(112, 120), zsb[:, 39:40], None, ALU.mult))
            pr.op("vector", (lambda v, t=t: v.scalar_tensor_tensor(
                dwh[:, t * 8:(t + 1) * 8], c_(80, 88), zsb[:, 35:36],
                c_(136, 144), ALU.mult, ALU.add)), inc="v")

    def ph_ffn():
        dwh_done = pr.mark("v")
        wd_ready = None
        for e in range(NE):
            for th in range(2):
                grp_g = [None] * 4
                grp_u = [None] * 4

                def issue_grp(g, e=e):
                    for kind, buf, dram, tgt in (("g", wg_buf, wgT_d, grp_g),
                                                 ("u", wu_buf, wuT_d, grp_u)):
                        par = g % 2
                        wt = [wbuf_free.get((kind, par))]
                        def f_dma(gp, e=e, g=g, buf=buf, dram=dram, par=par):
                            return gp.dma_start(
                                buf[par][:].rearrange("p (c f) -> p c f", c=NC),
                                dram[e, :, g * 512:(g + 1) * 512]
                                .rearrange("(c p) f -> p c f", p=128))
                        tgt[g] = pr.op("gpsimd", f_dma, waits=wt, inc="w", n=16)

                issue_grp(0)
                issue_grp(1)
                if th == 0:
                    wt = [st["ln_dump"], st["wd_lastrd"]]
                    if e == 0:
                        wt.append(st["regA_pe"])
                    for half in range(2):
                        def f_wd(gp, e=e, half=half):
                            return gp.dma_start(
                                wd_sb[:, half * 8192:(half + 1) * 8192]
                                .rearrange("p (c h) -> p c h", c=8),
                                wdT_d[e, half * 1024:(half + 1) * 1024, :]
                                .rearrange("(c p) h -> p c h", p=128))
                        wd_ready = pr.op("gpsimd", f_wd, waits=wt, inc="w", n=16)

                hm_done = [None] * NFT
                for ft in range(NFT):
                    if ft == 4:
                        issue_grp(2)
                    if ft == 8:
                        issue_grp(3)
                    g = ft // 4
                    par = g % 2
                    gpi = ft % 2
                    upi = 2 + (ft % 2)
                    fo = (ft % 4) * 128
                    wt = [grp_g[g], xn_tile_ready[th * 8 + 7]] + wb(pair(gpi), "f")
                    def f_gate(tn, gpi=gpi, par=par, fo=fo, th=th):
                        ins = None
                        for c in range(NC):
                            lhsT = wg_buf[par][:, c * 512 + fo: c * 512 + fo + 128]
                            for n in range(2):
                                ins = tn.matmul(
                                    ps[gpi][:, n * 512:(n + 1) * 512], lhsT,
                                    xn[:, c * S + th * 1024 + n * 512:
                                       c * S + th * 1024 + (n + 1) * 512],
                                    start=(c == 0), stop=(c == NC - 1))
                        return ins
                    mmg = pr.op("tensor", f_gate, waits=wt, inc="pe")
                    for b in pair(gpi):
                        bank_ready[b] = mmg
                    wt2 = [grp_u[g]] + wb(pair(upi), "f")
                    def f_up(tn, upi=upi, par=par, fo=fo, th=th):
                        ins = None
                        for c in range(NC):
                            lhsT = wu_buf[par][:, c * 512 + fo: c * 512 + fo + 128]
                            for n in range(2):
                                ins = tn.matmul(
                                    ps[upi][:, n * 512:(n + 1) * 512], lhsT,
                                    xn[:, c * S + th * 1024 + n * 512:
                                       c * S + th * 1024 + (n + 1) * 512],
                                    start=(c == 0), stop=(c == NC - 1))
                        return ins
                    mmu = pr.op("tensor", f_up, waits=wt2, inc="pe")
                    for b in pair(upi):
                        bank_ready[b] = mmu
                    if ft % 4 == 3:
                        done = pr.mark("pe")
                        wbuf_free[("g", par)] = done
                        wbuf_free[("u", par)] = done
                    sl = pr.op("scalar", (lambda s, gpi=gpi, ft=ft: s.activation(
                        silu_sb[:, (ft % 2) * 1024:(ft % 2 + 1) * 1024],
                        ps[gpi][:, 0:1024], AF.Silu)),
                        waits=[mmg, st["silu_free"][ft % 2]], inc="sc")
                    for b in pair(gpi):
                        bank_free[b] = sl
                    hm = pr.op("vector", (lambda v, upi=upi, ft=ft: v.tensor_tensor(
                        h_sb[:, ft * 1024:(ft + 1) * 1024],
                        silu_sb[:, (ft % 2) * 1024:(ft % 2 + 1) * 1024],
                        ps[upi][:, 0:1024], ALU.mult)), waits=[mmu, sl], inc="v")
                    for b in pair(upi):
                        bank_free[b] = hm
                    st["silu_free"][ft % 2] = hm
                    hm_done[ft] = hm

                for tt in range(8):
                    pi = tt % 4
                    gt = th * 8 + tt
                    wt = wb(pair(pi), "f") + [hm_done[NFT - 1], wd_ready]
                    def f_down(tn, pi=pi, tt=tt):
                        ins = None
                        for fc in range(16):
                            lhsT = h_sb[:, fc * 1024 + tt * 128: fc * 1024 + (tt + 1) * 128]
                            for n in range(2):
                                ins = tn.matmul(
                                    ps[pi][:, n * 512:(n + 1) * 512], lhsT,
                                    wd_sb[:, fc * 1024 + n * 512: fc * 1024 + (n + 1) * 512],
                                    start=(fc == 0), stop=(fc == 15))
                        return ins
                    mmd = pr.op("tensor", f_down, waits=wt, inc="pe")
                    for b in pair(pi):
                        bank_ready[b] = mmd
                    ev = pr.op("vector", (lambda v, pi=pi, gt=gt, e=e: v.scalar_tensor_tensor(
                        x_sb[:, gt * D:(gt + 1) * D], ps[pi][:, 0:1024],
                        dwh[:, gt * 8 + e: gt * 8 + e + 1],
                        x_sb[:, gt * D:(gt + 1) * D], ALU.mult, ALU.add)),
                        waits=[mmd, dwh_done], inc="v")
                    for b in pair(pi):
                        bank_free[b] = ev
                st["wd_lastrd"] = pr.mark("pe")
        st["regA_pe"] = pr.mark("pe")
        st["regB_pe"] = pr.mark("pe")
        st["regB_sc"] = pr.mark("sc")
        st["x_w"] = pr.mark("v")
        st["xn_pe"] = pr.mark("pe")
        for k in list(wbuf_free):
            wbuf_free[k] = None

    def ph_attn():
        ltr = pr.mark("pe")
        aw_ready = {}
        for name, dst, srcd in (("q", wq_sb, wqT_d), ("k", wk_sb, wkT_d),
                                ("v", wv_sb, wvT_d), ("o", wo_sb, woT_d)):
            def f_dma(gp, dst=dst, srcd=srcd):
                return gp.dma_start(dst[:].rearrange("p (c f) -> p c f", c=NC),
                                    srcd[:, :].rearrange("(c p) f -> p c f", p=128))
            aw_ready[name] = pr.op(
                "gpsimd", f_dma,
                waits=[st["regA_pe"], ltr, st["ln_dump"], st["wd_lastrd"]],
                inc="w", n=16)
        def f_bvt(tn):
            ins = None
            for c in range(NC):
                ins = tn.transpose(ps[2][0:1, c * 128:(c + 1) * 128],
                                   bpack[:, 16 + c:17 + c], identf[:])
            return ins
        bt = pr.op("tensor", f_bvt, waits=[LD_ALL] + wb([4, 5], "f"), inc="pe")
        brow = pr.op("vector", lambda v: v.tensor_copy(
            qf[0:1, 0:1024], ps[2][0:1, 0:1024]),
            waits=[bt, st["regB_pe"], st["regB_sc"]], inc="v")
        bank_free[4] = bank_free[5] = brow
        def f_bvb(tn):
            tn.matmul(ps[3][:, 0:512], onerow[:, :], qf[0:1, 0:512],
                      start=True, stop=True, skip_group_check=True)
            return tn.matmul(ps[3][:, 512:1024], onerow[:, :], qf[0:1, 512:1024],
                             start=True, stop=True, skip_group_check=True)
        bb = pr.op("tensor", f_bvb, waits=[brow] + wb([6, 7], "f"), inc="pe")
        bvb_done = pr.op("vector", lambda v: v.tensor_copy(bvb[:], ps[3][:, 0:1024]),
                         waits=[bb], inc="v")
        bank_free[6] = bank_free[7] = bvb_done

        kslot_ready = [None] * 5
        vslot_ready = [None] * 5
        kslot_lastrd = [None] * 5
        vslot_lastrd = [None] * 5
        qf_lastrd = None
        e_free = None
        e_done = {"last": None}
        pend = {}

        def back_half(tn_state):
            T = tn_state["T"]
            pvl = tn_state["pv_last"]
            rz = tn_state["rz"]
            cv = None
            for h in range(NH):
                cv = pr.op("vector", (lambda v, h=h: v.tensor_scalar(
                    ctm[:, h * 64:(h + 1) * 64], ps[2][:, h * 64:(h + 1) * 64],
                    zsb[:, 16 + h:17 + h], None, ALU.mult)),
                    waits=[pvl, rz], inc="v")
            bank_free[4] = bank_free[5] = cv
            def f_ctr(tn):
                ins = None
                for c in range(NC):
                    ins = tn.transpose(psb[3][:, c * 128:(c + 1) * 128],
                                       ctm[:, c * 128:(c + 1) * 128], identb[:])
                return ins
            ctr = pr.op("tensor", f_ctr, waits=wb([6], "f") + [cv], inc="pe")
            bank_ready[6] = ctr
            cfm_cp = pr.op("vector", lambda v: v.tensor_copy(
                cfm[:], psb[3][:, 0:1024]), waits=[ctr], inc="v")
            bank_free[6] = cfm_cp
            def f_o(tn):
                ins = None
                for c in range(NC):
                    lhsT = cfm[:, c * 128:(c + 1) * 128]
                    for n in range(2):
                        ins = tn.matmul(ps[1][:, n * 512:(n + 1) * 512], lhsT,
                                        wo_sb[:, c * 1024 + n * 512: c * 1024 + (n + 1) * 512],
                                        start=(c == 0), stop=(c == NC - 1))
                return ins
            mo = pr.op("tensor", f_o,
                       waits=wb([2, 3], "f") + [cfm_cp, aw_ready["o"]], inc="pe")
            bank_ready[2] = bank_ready[3] = mo
            oev = pr.op("vector", (lambda v, T=T: v.tensor_tensor(
                x_sb[:, T * D:(T + 1) * D], ps[1][:, 0:1024],
                x_sb[:, T * D:(T + 1) * D], ALU.add)), waits=[mo], inc="v")
            bank_free[2] = bank_free[3] = oev

        for ch in range(4):
            tiles = list(range(4 * ch, 4 * ch + 4))
            tok0 = 4 * ch * 128
            if pend:
                back_half(pend)
                pend = {}
            for m in range(8):
                bk = m % 2
                wt = [aw_ready["q"], xn_tile_ready[tiles[-1]], qf_lastrd] + wb([bk], "f")
                def f_q(tn, m=m, bk=bk, tok0=tok0):
                    ins = None
                    for c in range(NC):
                        ins = tn.matmul(ps[0][:, bk * 512:(bk + 1) * 512],
                                        wq_sb[:, c * 1024 + m * 128: c * 1024 + (m + 1) * 128],
                                        xn[:, c * S + tok0: c * S + tok0 + 512],
                                        start=(c == 0), stop=(c == NC - 1))
                    return ins
                mq = pr.op("tensor", f_q, waits=wt, inc="pe")
                bank_ready[bk] = mq
                evq = pr.op("vector", (lambda v, m=m, bk=bk: v.tensor_scalar(
                    qf[:, m * 512:(m + 1) * 512], ps[0][:, bk * 512:(bk + 1) * 512],
                    bpack[:, m:m + 1], None, ALU.add)), waits=[mq], inc="v")
                bank_free[bk] = evq
            qf_ready = pr.mark("v")
            for m in range(8):
                bk = 2 + m % 2
                wt = [aw_ready["k"]] + wb([bk], "f")
                if m < 2:
                    wt += [kslot_lastrd[T % 5] for T in tiles]
                def f_k(tn, m=m, bk=bk, tok0=tok0):
                    ins = None
                    for c in range(NC):
                        ins = tn.matmul(ps[1][:, (bk - 2) * 512:(bk - 1) * 512],
                                        wk_sb[:, c * 1024 + m * 128: c * 1024 + (m + 1) * 128],
                                        xn[:, c * S + tok0: c * S + tok0 + 512],
                                        start=(c == 0), stop=(c == NC - 1))
                    return ins
                mk = pr.op("tensor", f_k, waits=wt, inc="pe")
                bank_ready[bk] = mk
                evk = None
                for i, T in enumerate(tiles):
                    sl = T % 5
                    evk = pr.op("vector", (lambda v, m=m, bk=bk, i=i, sl=sl: v.tensor_scalar(
                        kf[:, sl * 1024 + m * 128: sl * 1024 + (m + 1) * 128],
                        ps[1][:, (bk - 2) * 512 + i * 128:(bk - 2) * 512 + (i + 1) * 128],
                        bpack[:, 8 + m:9 + m], None, ALU.add)), waits=[mk], inc="v")
                bank_free[bk] = evk
            for T in tiles:
                kslot_ready[T % 5] = pr.mark("v")
            for i, T in enumerate(tiles):
                sl = T % 5
                wt = [aw_ready["v"]] + wb([4, 5], "f")
                def f_v(tn, i=i, tok0=tok0):
                    ins = None
                    for c in range(NC):
                        lhsT = xn[:, c * S + tok0 + i * 128: c * S + tok0 + (i + 1) * 128]
                        for n in range(2):
                            ins = tn.matmul(ps[2][:, n * 512:(n + 1) * 512], lhsT,
                                            wv_sb[:, c * 1024 + n * 512: c * 1024 + (n + 1) * 512],
                                            start=(c == 0), stop=(c == NC - 1))
                    return ins
                mv = pr.op("tensor", f_v, waits=wt, inc="pe")
                bank_ready[4] = bank_ready[5] = mv
                evv = pr.op("vector", (lambda v, sl=sl: v.tensor_tensor(
                    vt[:, sl * 1024:(sl + 1) * 1024], ps[2][:, 0:1024], bvb[:],
                    ALU.add)), waits=[mv, bvb_done, vslot_lastrd[sl]], inc="v")
                bank_free[4] = bank_free[5] = evv
                vslot_ready[sl] = evv
            for T in tiles:
                iic = T - 4 * ch
                epar = (T % 2) * 1024
                jts = ([(0, (T - 1) % 5)] if T > 0 else []) + [(1, T % 5)]
                z_open = False
                pv_open = {4: False, 5: False}
                zmm_last = None
                pv_last = None
                for hg in range(4):
                    wt = wb([0, 1], "f") + \
                        [kslot_ready[sl] for (_, sl) in jts] + [qf_ready]
                    def f_sc(tn, hg=hg, jts=jts, iic=iic):
                        ins = None
                        for hh in range(4):
                            h = 4 * hg + hh
                            po = (hh % 2) * 64
                            for (jt, sl) in jts:
                                col = (hh % 2) * 512 + (hh // 2) * 256 + jt * 128
                                ins = tn.matmul(
                                    ps[0][:, col:col + 128],
                                    kf[po:po + 64, sl * 1024 + (h // 2) * 128:
                                       sl * 1024 + (h // 2 + 1) * 128],
                                    qf[po:po + 64, (h // 2) * 512 + iic * 128:
                                       (h // 2) * 512 + (iic + 1) * 128],
                                    start=True, stop=True,
                                    tile_position=(po, 0), skip_group_check=True)
                        return ins
                    msc = pr.op("tensor", f_sc, waits=wt, inc="pe")
                    bank_ready[0] = bank_ready[1] = msc
                    exl = None
                    for par in range(2):
                        if T > 0:
                            ma = pr.op("vector", (lambda v, par=par: v.tensor_tensor(
                                stmp[:], ps[0][:, par * 512:(par + 1) * 512],
                                maskwA[:], ALU.add)),
                                waits=[msc, LD_ALL, e_free, e_done.get("last")], inc="v")
                            ex = pr.op("scalar", (lambda s, par=par, epar=epar: s.activation(
                                E_sb[:, epar + par * 512: epar + (par + 1) * 512],
                                stmp[:], AF.Exp, scale=SCALE)),
                                waits=[ma, e_free, st["regB_pe"]], inc="sc")
                        else:
                            ma = pr.op("vector", (lambda v, par=par: v.tensor_tensor(
                                stmp[:, 0:256].rearrange("p (a f) -> p a f", a=2),
                                bass.AP(ps[0], par * 512 + 128,
                                        [[1024, 128], [256, 2], [1, 128]]),
                                maskwB[:].rearrange("p (a f) -> p a f", a=2), ALU.add)),
                                waits=[msc, LD_ALL, e_free, e_done.get("last")], inc="v")
                            ex = pr.op("scalar", (lambda s, par=par, epar=epar: s.activation(
                                bass.AP(E_sb, epar + par * 512 + 128,
                                        [[2048, 128], [256, 2], [1, 128]]),
                                stmp[:, 0:256].rearrange("p (a f) -> p a f", a=2),
                                AF.Exp, scale=SCALE)),
                                waits=[ma, e_free, st["regB_pe"]], inc="sc")
                        e_done["last"] = ex
                        exl = ex
                    bank_free[0] = bank_free[1] = e_done["last"]
                    for (jt, _) in jts:
                        e_done[jt] = exl
                    zplan = []
                    pvplan = []
                    for hh in range(4):
                        h = 4 * hg + hh
                        bnk = 4 + (h // 8)
                        ecol = epar + (hh % 2) * 512 + (hh // 2) * 256
                        for k, (jt, sl) in enumerate(jts):
                            zst = not z_open
                            z_open = True
                            zsp = (hg == 3 and hh == 3 and k == len(jts) - 1)
                            zplan.append((h, ecol + jt * 128, zst, zsp))
                            pst = not pv_open[bnk]
                            pv_open[bnk] = True
                            psp = (hh == 3 and k == len(jts) - 1 and hg % 2 == 1)
                            pvplan.append((h, ecol + jt * 128, sl, pst, psp))
                    def f_z(tn, zplan=tuple(zplan)):
                        ins = None
                        for (h, ec, zst, zsp) in zplan:
                            ins = tn.matmul(
                                ps[3][:, 512 + h:513 + h],
                                E_sb[:, ec:ec + 128],
                                onecol[:], start=zst, stop=zsp,
                                skip_group_check=True)
                        return ins
                    zwt = [e_done[jt] for (jt, _) in jts]
                    if hg == 0:
                        zwt += wb([7], "f")
                    zmm_last = pr.op("tensor", f_z, waits=zwt, inc="pe")
                    if PIPE_ATTN and hg == 0 and pend:
                        back_half(pend)
                        pend = {}
                    def f_pv(tn, pvplan=tuple(pvplan)):
                        ins = None
                        for (h, ec, sl, pst, psp) in pvplan:
                            ins = tn.matmul(
                                ps[2][:, h * 64:(h + 1) * 64],
                                E_sb[:, ec:ec + 128],
                                vt[:, sl * 1024 + h * 64: sl * 1024 + (h + 1) * 64],
                                start=pst, stop=psp, skip_group_check=True)
                        return ins
                    pvwt = [e_done[jt] for (jt, _) in jts] + \
                           [vslot_ready[sl] for (_, sl) in jts]
                    if hg == 0:
                        pvwt += wb([4, 5], "f")
                    pv_last = pr.op("tensor", f_pv, waits=pvwt, inc="pe")
                    e_free = pr.mark("pe")
                bank_ready[7] = zmm_last
                bank_ready[4] = bank_ready[5] = pv_last
                for (jt, sl) in jts:
                    kslot_lastrd[sl] = pr.mark("pe")
                    vslot_lastrd[sl] = pr.mark("pe")
                qf_lastrd = pr.mark("pe")
                zev = pr.op("vector", lambda v: v.tensor_copy(
                    zsb[:, 0:16], ps[3][:, 512:528]), waits=[zmm_last], inc="v")
                bank_free[7] = zev
                rz = pr.op("vector", lambda v: v.reciprocal(
                    zsb[:, 16:32], zsb[:, 0:16]), inc="v")
                pend = {"T": T, "pv_last": pv_last, "rz": rz}
                if not PIPE_ATTN:
                    back_half(pend)
                    pend = {}
        if pend:
            back_half(pend)
        sxl = None
        for c in range(NC):
            sxl = pr.op("vector", (lambda v, c=c: v.reduce_sum(
                zsb[:, 40 + c:41 + c], xn[:, c * S:(c + 1) * S], axis=AX.X)),
                inc="v")
        sxbf = pr.op("vector", lambda v: v.tensor_copy(
            wq_sb[:, 5128:5136], zsb[:, 40:48]), waits=[sxl], inc="v")
        def f_sv(tn):
            ins = None
            for m in range(8):
                for c in range(NC):
                    ins = tn.matmul(ps[3][:, m:m + 1],
                                    wv_sb[:, c * 1024 + m * 128: c * 1024 + (m + 1) * 128],
                                    wq_sb[:, 5128 + c:5129 + c],
                                    start=(c == 0), stop=(c == NC - 1),
                                    skip_group_check=True)
            return ins
        msv = pr.op("tensor", f_sv, waits=[sxbf] + wb([6], "f"), inc="pe")
        sv_ev = pr.op("vector", lambda v: v.scalar_tensor_tensor(
            zsb[:, 32:40], bpack[:, 16:24], float(S), ps[3][:, 0:8],
            ALU.mult, ALU.add), waits=[msv], inc="v")
        bank_free[6] = sv_ev
        svbf = pr.op("vector", lambda v: v.tensor_copy(
            wq_sb[:, 5120:5128], zsb[:, 32:40]), inc="v")
        def f_corr(tn):
            ins = None
            for c in range(NC):
                for n in range(2):
                    ins = tn.matmul(ps[3][0:1, n * 512:(n + 1) * 512],
                                    wq_sb[:, 5120 + c:5121 + c],
                                    wo_sb[:, c * 1024 + n * 512: c * 1024 + (n + 1) * 512],
                                    start=(c == 0), stop=(c == NC - 1),
                                    skip_group_check=True)
            return ins
        mcor = pr.op("tensor", f_corr, waits=[svbf] + wb([6, 7], "f"), inc="pe")
        def f_bot(tn):
            ins = None
            for c in range(NC):
                ins = tn.transpose(ps[2][0:1, c * 128:(c + 1) * 128],
                                   bpack[:, 24 + c:25 + c], identf[:])
            return ins
        mbo = pr.op("tensor", f_bot, waits=wb([4, 5], "f"), inc="pe")
        wqF = wq_sb.bitcast(F32)
        r1 = pr.op("vector", lambda v: v.tensor_scalar(
            wqF[0:1, 0:1024], ps[3][0:1, 0:1024], 1.0e-9, None, ALU.mult),
            waits=[mcor], inc="v")
        r2 = pr.op("vector", lambda v: v.tensor_copy(
            wqF[0:1, 1024:2048], ps[2][0:1, 0:1024]), waits=[mbo], inc="v")
        bank_free[4] = bank_free[5] = r2
        bank_free[6] = bank_free[7] = r1
        r3 = pr.op("vector", lambda v: v.tensor_tensor(
            wq_sb[0:1, 6144:7168], wqF[0:1, 0:1024], wqF[0:1, 1024:2048], ALU.add),
            inc="v")
        def f_cb(tn):
            tn.matmul(ps[0][:, 0:512], onerow[:, :], wq_sb[0:1, 6144:6656],
                      start=True, stop=True, skip_group_check=True)
            return tn.matmul(ps[0][:, 512:1024], onerow[:, :], wq_sb[0:1, 6656:7168],
                             start=True, stop=True, skip_group_check=True)
        mcb = pr.op("tensor", f_cb, waits=[r3] + wb([0, 1], "f"), inc="pe")
        bank_ready[0] = bank_ready[1] = mcb
        last = None
        for t in range(NT):
            last = pr.op("vector", (lambda v, t=t: v.tensor_tensor(
                x_sb[:, t * D:(t + 1) * D], ps[0][:, 0:1024],
                x_sb[:, t * D:(t + 1) * D], ALU.add)), waits=[mcb], inc="v")
        bank_free[0] = bank_free[1] = last
        st["regA_pe"] = pr.mark("pe")
        st["regB_pe"] = pr.mark("pe")
        st["regB_sc"] = pr.mark("sc")
        st["x_w"] = pr.mark("v")
        st["xn_pe"] = pr.mark("pe")
        st["wd_lastrd"] = pr.mark("pe")
        for k in ("g", "u"):
            for p_ in range(2):
                wbuf_free[(k, p_)] = st["regB_pe"]

    def ph_store():
        def f_st(sy):
            return sy.dma_start(
                out_d[:, :].rearrange("(t p) h -> p t h", p=128),
                x_sb[:].rearrange("p (t h) -> p t h", t=NT))
        pr.op("sync", f_st, waits=[st["x_w"]], inc="st", n=16)
        if debug_dump:
            dxn = nc.dram_tensor("dbg_xn", [128, NC * S], BF16, kind="ExternalOutput")
            dh = nc.dram_tensor("dbg_h", [128, NFT * 1024], BF16, kind="ExternalOutput")
            ddw = nc.dram_tensor("dbg_dwh", [128, 128], F32, kind="ExternalOutput")
            dlg = nc.dram_tensor("dbg_lg", [128, 128], F32, kind="ExternalOutput")
            dst_ = nc.dram_tensor("dbg_stats", [128, 128], F32, kind="ExternalOutput")
            wfin = [st["x_w"], pr.mark("pe"), pr.mark("v"), pr.mark("sc")]
            pr.op("sync", lambda sy: sy.dma_start(dxn[:, :], xn[:]),
                  waits=wfin, inc="st", n=16)
            pr.op("sync", lambda sy: sy.dma_start(dh[:, :], h_sb[:]), inc="st", n=16)
            pr.op("sync", lambda sy: sy.dma_start(ddw[:, :], dwh[:]), inc="st", n=16)
            pr.op("sync", lambda sy: sy.dma_start(dlg[:, :], Lg[:]), inc="st", n=16)
            pr.op("sync", lambda sy: sy.dma_start(dst_[:, :], stats[:]), inc="st", n=16)
            drs = nc.dram_tensor("dbg_rsc", [128, 160], F32, kind="ExternalOutput")
            dzs = nc.dram_tensor("dbg_zsb", [128, 48], F32, kind="ExternalOutput")
            pr.op("sync", lambda sy: sy.dma_start(drs[:, :], rsc[:]), inc="st", n=16)
            pr.op("sync", lambda sy: sy.dma_start(dzs[:, :], zsb[:]), inc="st", n=16)

    ph_load()
    ph_ln(with_route=True)
    ph_route()
    if "ffn1" in stages:
        ph_ffn()
    if "attn" in stages:
        ph_ln()
        ph_attn()
    if "ffn2" in stages:
        ph_ln(with_route=True)
        ph_route()
        ph_ffn()
    ph_store()

    with (
        nc.semaphore("s_ld") as s_ld,
        nc.semaphore("s_w") as s_w,
        nc.semaphore("s_pe") as s_pe,
        nc.semaphore("s_v") as s_v,
        nc.semaphore("s_sc") as s_sc,
        nc.semaphore("s_st") as s_st,
        nc.semaphore("s_vv") as s_vv,
        nc.Block() as block,
    ):
        sems = {"ld": s_ld, "w": s_w, "pe": s_pe, "v": s_v, "sc": s_sc, "st": s_st}

        def replay(name, tail=None):
            serial = name == "vector"

            def go(eng):
                lastw = {}
                cnt2 = {}
                prev = None
                for fn, waits, inc, n, nospace in pr.streams[name]:
                    if serial and prev is not None and not nospace:
                        eng.wait_ge(prev[0], prev[1])
                    for (s, val) in waits:
                        if lastw.get(s, 0) < val:
                            eng.wait_ge(sems[s], val)
                            lastw[s] = val
                    ins = fn(eng)
                    if inc is not None:
                        assert ins is not None, name
                        ins.then_inc(sems[inc], n)
                        cnt2[inc] = cnt2.get(inc, 0) + n
                        if serial:
                            prev = (sems[inc], cnt2[inc])
                    elif serial:
                        assert ins is not None, f"{name} op returned None"
                        ins.then_inc(s_vv, 1)
                        cnt2["vv"] = cnt2.get("vv", 0) + 1
                        prev = (s_vv, cnt2["vv"])
                if tail is not None:
                    tail(eng)
            return go

        block.tensor(replay("tensor"))
        block.vector(replay("vector"))
        block.scalar(replay("scalar"))
        block.gpsimd(replay("gpsimd"))
        block.sync(replay("sync", tail=lambda sy: sy.wait_ge(s_st, pr.cnt["st"])))

    return nc


_CACHE = {}


def _host_consts():
    ident_b = np.eye(128, dtype=ml_dtypes.bfloat16)
    ident_f = np.eye(128, dtype=np.float32)
    jj = np.arange(128)[:, None]
    ii = np.arange(128)[None, :]
    m1 = np.where(jj <= ii, 0.0, NEGBIG).astype(ml_dtypes.bfloat16)
    m0 = np.where(jj >= ii, 0.0, NEGBIG).astype(ml_dtypes.bfloat16)
    maskwA = np.ascontiguousarray(np.tile(np.concatenate([m0, m1], axis=1), (1, 2)))
    maskwB = np.ascontiguousarray(np.tile(m1, (1, 2)))
    onecol = np.ones((128, 1), dtype=ml_dtypes.bfloat16)
    onerow = np.ones((1, 128), dtype=ml_dtypes.bfloat16)
    return ident_b, ident_f, maskwA, maskwB, onecol, onerow


def kernel(L0_attn_x, L0_attn_Wq, L0_attn_bq, L0_attn_Wk, L0_attn_bk,
           L0_attn_Wv, L0_attn_bv, L0_attn_Wo, L0_attn_bo,
           L0_ffn_router_w, L0_ffn_w_gate, L0_ffn_w_up, L0_ffn_w_down,
           _stages=ALL_STAGES):
    x = np.ascontiguousarray(np.asarray(L0_attn_x, dtype=np.float32))
    B = x.shape[0]
    wgT = np.ascontiguousarray(np.asarray(L0_ffn_w_gate).transpose(0, 2, 1))
    wuT = np.ascontiguousarray(np.asarray(L0_ffn_w_up).transpose(0, 2, 1))
    wdT = np.ascontiguousarray(np.asarray(L0_ffn_w_down).transpose(0, 2, 1))
    wqT = np.ascontiguousarray(np.asarray(L0_attn_Wq).T)
    wkT = np.ascontiguousarray(np.asarray(L0_attn_Wk).T)
    wvT = np.ascontiguousarray(np.asarray(L0_attn_Wv).T)
    woT = np.ascontiguousarray(np.asarray(L0_attn_Wo).T)
    rTp = np.ascontiguousarray(
        np.asarray(L0_ffn_router_w).reshape(NE, NC, 128).transpose(2, 1, 0)
        .reshape(128, NC * NE))
    bpack = np.ascontiguousarray(np.concatenate(
        [np.asarray(b, dtype=np.float32).reshape(8, 128).T
         for b in (L0_attn_bq, L0_attn_bk, L0_attn_bv, L0_attn_bo)], axis=1))
    ident_b, ident_f, maskwA, maskwB, onecol, onerow = _host_consts()

    key = ("prog", tuple(_stages))
    if key not in _CACHE:
        _CACHE[key] = build_program(_stages)
    nc = _CACHE[key]

    shared = {"wgT": wgT, "wuT": wuT, "wdT": wdT, "wqT": wqT, "wkT": wkT,
              "wvT": wvT, "woT": woT, "rT": rTp, "bpack": bpack,
              "identb": ident_b, "identf": ident_f,
              "maskwA": maskwA, "maskwB": maskwB,
              "onecol": onecol, "onerow": onerow}
    in_maps = [dict(shared, x=np.ascontiguousarray(x[c])) for c in range(B)]
    res = run_bass_kernel_spmd(nc, in_maps, core_ids=list(range(B)))
    out = np.stack([np.asarray(res.results[c]["out"]) for c in range(B)], axis=0)
    return out.astype(np.float32)


# version 08
# speedup vs baseline: 1.2060x; 1.2060x over previous
import numpy as np
import ml_dtypes
import concourse.bass as bass
import concourse.mybir as mybir
from concourse.bass_utils import run_bass_kernel_spmd

F32 = mybir.dt.float32
BF16 = mybir.dt.bfloat16
AF = mybir.ActivationFunctionType
ALU = mybir.AluOpType
AX = mybir.AxisListType

D = 1024
S = 2048
NT = 16
NC = 8
NE = 8
FF = 2048
NFT = 16
NH = 16
SCALE = 0.125
NEGBIG = -1.0e6
EPS = 1e-5

W_LN = 4
W_PRE = 60
W_PREATT = 40
W_ATT = 10

ENGS = ("tensor", "vector", "scalar", "sync", "gpsimd")
ALL_STAGES = ("ffn1", "attn", "ffn2")


class Prog:

    def __init__(self):
        self.streams = {e: [] for e in ENGS}
        self.cnt = {s: 0 for s in ("ld", "w", "pe", "v", "sc", "st")}

    def op(self, eng, fn, waits=(), inc=None, n=1, nospace=False):
        ws = [w for w in waits if w is not None]
        self.streams[eng].append((fn, ws, inc, n, nospace))
        if inc is not None:
            self.cnt[inc] += n
            return (inc, self.cnt[inc])
        return None

    def mark(self, sem):
        return (sem, self.cnt[sem])


def build_program(stages=ALL_STAGES, debug_dump=False, ATTN_SIMPLE=False, ATTN_LEVEL=9, PIPE_ATTN=True):
    nc = bass.Bass("TRN2", target_bir_lowering=False, detect_race_conditions=False)

    x_d = nc.dram_tensor("x", [S, D], F32, kind="ExternalInput")
    wgT_d = nc.dram_tensor("wgT", [NE, D, FF], F32, kind="ExternalInput")
    wuT_d = nc.dram_tensor("wuT", [NE, D, FF], F32, kind="ExternalInput")
    wdT_d = nc.dram_tensor("wdT", [NE, FF, D], F32, kind="ExternalInput")
    wqT_d = nc.dram_tensor("wqT", [D, D], F32, kind="ExternalInput")
    wkT_d = nc.dram_tensor("wkT", [D, D], F32, kind="ExternalInput")
    wvT_d = nc.dram_tensor("wvT", [D, D], F32, kind="ExternalInput")
    woT_d = nc.dram_tensor("woT", [D, D], F32, kind="ExternalInput")
    rT_d = nc.dram_tensor("rT", [128, 64], F32, kind="ExternalInput")
    bpack_d = nc.dram_tensor("bpack", [128, 32], F32, kind="ExternalInput")
    identb_d = nc.dram_tensor("identb", [128, 128], BF16, kind="ExternalInput")
    identf_d = nc.dram_tensor("identf", [128, 128], F32, kind="ExternalInput")
    maskwA_d = nc.dram_tensor("maskwA", [128, 512], BF16, kind="ExternalInput")
    maskwB_d = nc.dram_tensor("maskwB", [128, 256], BF16, kind="ExternalInput")
    onecol_d = nc.dram_tensor("onecol", [128, 1], BF16, kind="ExternalInput")
    onerow_d = nc.dram_tensor("onerow", [1, 128], BF16, kind="ExternalInput")
    out_d = nc.dram_tensor("out", [S, D], F32, kind="ExternalOutput")

    x_sb = nc.alloc_sbuf_tensor("x_sb", [128, NT * D], F32)
    xn = nc.alloc_sbuf_tensor("xn", [128, NC * S], BF16)
    regA = nc.alloc_sbuf_tensor("regA", [128, 65536], mybir.dt.uint8)
    a0 = nc.lookup_mloc(regA).addr
    h_sb = nc.alloc_sbuf_tensor_at("h_sb", [128, NFT * 1024], BF16, offset=a0)
    wd_sb = nc.alloc_sbuf_tensor_at("wd_sb", [128, 16 * D], BF16, offset=a0 + 32768)
    wq_sb = nc.alloc_sbuf_tensor_at("wq_sb", [128, NC * D], BF16, offset=a0)
    wk_sb = nc.alloc_sbuf_tensor_at("wk_sb", [128, NC * D], BF16, offset=a0 + 16384)
    wv_sb = nc.alloc_sbuf_tensor_at("wv_sb", [128, NC * D], BF16, offset=a0 + 32768)
    wo_sb = nc.alloc_sbuf_tensor_at("wo_sb", [128, NC * D], BF16, offset=a0 + 49152)
    regB = nc.alloc_sbuf_tensor("regB", [128, 40960], mybir.dt.uint8)
    b0 = nc.lookup_mloc(regB).addr
    wg_buf = [nc.alloc_sbuf_tensor_at(f"wg{p}", [128, NC * 512], BF16,
                                      offset=b0 + p * 8192) for p in range(2)]
    wu_buf = [nc.alloc_sbuf_tensor_at(f"wu{p}", [128, NC * 512], BF16,
                                      offset=b0 + 16384 + p * 8192) for p in range(2)]
    silu_sb = nc.alloc_sbuf_tensor_at("silu_sb", [128, 2 * 1024], BF16, offset=b0 + 32768)
    qf = nc.alloc_sbuf_tensor_at("qf", [128, 8 * 512], BF16, offset=b0)
    kf = nc.alloc_sbuf_tensor_at("kf", [128, 5 * 1024], BF16, offset=b0 + 8192)
    vt = nc.alloc_sbuf_tensor_at("vt", [128, 5 * 1024], BF16, offset=b0 + 18432)
    E_sb = nc.alloc_sbuf_tensor_at("E_sb", [128, 2048], BF16, offset=b0 + 28672)
    ctm = nc.alloc_sbuf_tensor_at("ctm", [128, 1024], BF16, offset=b0 + 32768)
    cfm = nc.alloc_sbuf_tensor_at("cfm", [128, 1024], BF16, offset=b0 + 34816)
    bvb = nc.alloc_sbuf_tensor_at("bvb", [128, 1024], F32, offset=b0 + 36864)
    identb = nc.alloc_sbuf_tensor("identb_sb", [128, 128], BF16)
    identf = nc.alloc_sbuf_tensor("identf_sb", [128, 128], F32)
    maskwA = nc.alloc_sbuf_tensor("maskwA_sb", [128, 512], BF16)
    maskwB = nc.alloc_sbuf_tensor("maskwB_sb", [128, 256], BF16)
    bpack = nc.alloc_sbuf_tensor("bpack_sb", [128, 32], F32)
    rT = nc.alloc_sbuf_tensor("rT_sb", [128, 64], BF16)
    onecol = nc.alloc_sbuf_tensor("onecol_sb", [128, 1], BF16)
    onerow = nc.alloc_sbuf_tensor("onerow_sb", [1, 128], BF16)
    stats = nc.alloc_sbuf_tensor("stats", [128, 128], F32)
    Lg = nc.alloc_sbuf_tensor("Lg", [128, 128], F32)
    dwh = nc.alloc_sbuf_tensor("dwh", [128, 128], F32)
    rsc = nc.alloc_sbuf_tensor("rsc", [128, 160], F32)
    zsb = nc.alloc_sbuf_tensor("zsb", [128, 48], F32)
    spacer = nc.alloc_sbuf_tensor("spacer", [1, 4], F32)
    stmp = nc.alloc_sbuf_tensor("stmp", [128, 512], F32)

    ST_MS, ST_QS, ST_M, ST_V, ST_R = 0, 16, 32, 48, 64

    ps = [nc.alloc_psum_tensor(f"ps{i}", [128, 1024], F32) for i in range(4)]
    psb = [p.bitcast(BF16) for p in ps]

    pr = Prog()

    bank_ready = [None] * 8
    bank_free = [None] * 8
    wbuf_free = {}
    xn_tile_ready = [None] * NT
    st = {"xn_pe": None, "x_w": None, "regA_pe": None, "regB_pe": None,
          "regB_sc": None, "ln_dump": None, "wd_lastrd": None,
          "silu_free": [None, None], "xtw": [None] * NT, "hm_last": None}

    def pair(pi):
        return (2 * pi, 2 * pi + 1)

    def wb(bks, which):
        arr = bank_ready if which == "r" else bank_free
        return [arr[b] for b in bks if arr[b] is not None]

    def warm(n):
        if n <= 0:
            return
        def f_warm(tn):
            ins = None
            for _ in range(n):
                ins = tn.matmul(ps[3][:, 0:512], identb[:], maskwA[:],
                                start=True, stop=True, skip_group_check=True)
            return ins
        pr.op("tensor", f_warm, waits=wb([6], "f") + [LD_ALL], inc="pe")

    def ph_load():
        for k in range(4):
            def f_x(sy, k=k):
                return sy.dma_start(
                    bass.AP(x_sb, k * 4 * D, [[NT * D, 128], [D, 4], [1, D]]),
                    x_d[k * 512:(k + 1) * 512, :].rearrange(
                        "(t p) h -> p t h", p=128))
            pr.op("sync", f_x, inc="ld", n=16)
        for (dst, src) in ((identb, identb_d), (identf, identf_d),
                           (maskwA, maskwA_d), (maskwB, maskwB_d),
                           (bpack, bpack_d), (onecol, onecol_d), (onerow, onerow_d)):
            pr.op("sync", (lambda sy, d=dst, s=src: sy.dma_start(d[:], s[:, :])),
                  inc="ld", n=16)
        pr.op("gpsimd", lambda g: g.dma_start(rT[:], rT_d[:, :]), inc="w", n=16)

    LD_ALL = ("ld", 16 * 11)
    W_RT = ("w", 16)

    def ph_ln(with_route=False, first=False, regb_stage=False):
        for t in range(NT):
            xt = x_sb[:, t * D:(t + 1) * D]
            if first:
                swt = [("ld", 16 * (t // 4 + 1))]
            else:
                swt = [st["xtw"][t]]
            pr.op("vector", (lambda v, xt=xt, t=t: v.reduce_sum(
                stats[:, ST_MS + t:ST_MS + t + 1], xt, axis=AX.X)),
                waits=swt)
            if regb_stage:
                dmp = silu_sb[:, 0:1024]
                dwt = swt + [st["hm_last"]]
            else:
                dmp = wd_sb[:, t * 1024:(t + 1) * 1024]
                dwt = swt + [st["wd_lastrd"]]
            pr.op("scalar", (lambda s, xt=xt, t=t, dmp=dmp: s.activation(
                dmp, xt, AF.Square,
                accum_out=stats[:, ST_QS + t:ST_QS + t + 1])),
                waits=dwt, inc="sc")
        st["ln_dump"] = pr.mark("sc")
        sq_done = pr.mark("sc")
        pr.op("vector", lambda v: v.tensor_scalar(
            stats[:, ST_M:ST_M + 16], stats[:, ST_MS:ST_MS + 16], 1.0 / D, None, ALU.mult))
        pr.op("vector", lambda v: v.tensor_scalar(
            stats[:, ST_V:ST_V + 16], stats[:, ST_QS:ST_QS + 16], 1.0 / D, None, ALU.mult),
            waits=[sq_done])
        pr.op("vector", lambda v: v.tensor_tensor(
            rsc[:, 0:16], stats[:, ST_M:ST_M + 16], stats[:, ST_M:ST_M + 16], ALU.mult))
        pr.op("vector", lambda v: v.tensor_tensor(
            stats[:, ST_V:ST_V + 16], stats[:, ST_V:ST_V + 16], rsc[:, 0:16],
            ALU.subtract))
        vvar = pr.op("vector", lambda v: v.tensor_scalar(
            stats[:, ST_V:ST_V + 16], stats[:, ST_V:ST_V + 16], EPS, None,
            ALU.add), inc="v")
        vstd = pr.op("scalar", lambda s: s.activation(
            rsc[:, 16:32], stats[:, ST_V:ST_V + 16], AF.Sqrt),
            waits=[vvar], inc="sc")
        vrstd = pr.op("vector", lambda v: v.reciprocal(
            stats[:, ST_R:ST_R + 16], rsc[:, 16:32]), waits=[vstd], inc="v")
        tr_done = [None, None]
        bvbB = bvb.bitcast(BF16)
        for t in range(NT):
            if regb_stage:
                stage = bvbB[:, (t % 2) * 1024:(t % 2 + 1) * 1024]
                stwt = [vrstd, st["regB_pe"], tr_done[t % 2]]
            else:
                stage = h_sb[:, t * 1024:(t + 1) * 1024]
                stwt = [vrstd, st["regA_pe"]]
            xt = x_sb[:, t * D:(t + 1) * D]
            vst = pr.op("vector", (lambda v, stage=stage, xt=xt, t=t: v.tensor_scalar(
                stage, xt, stats[:, ST_M + t:ST_M + t + 1],
                stats[:, ST_R + t:ST_R + t + 1], ALU.subtract, ALU.mult)),
                waits=stwt, inc="v")
            pk = 6 + (t % 2)
            half = (t % 2) * 1024
            def f_tr(tn, stage=stage, half=half):
                ins = None
                for c in range(NC):
                    ins = tn.transpose(psb[3][:, half + c * 128:half + (c + 1) * 128],
                                       stage[:, c * 128:(c + 1) * 128], identb[:])
                return ins
            tr = pr.op("tensor", f_tr, waits=[vst, LD_ALL] + wb([pk], "f"), inc="pe")
            if regb_stage:
                tr_done[t % 2] = tr
            bank_ready[pk] = tr
            def f_cp(v, t=t, half=half):
                dst = bass.AP(xn, t * 128, [[NC * S, 128], [S, NC], [1, 128]])
                return v.tensor_copy(dst, psb[3][:, half:half + 1024]
                                     .rearrange("p (c f) -> p c f", c=NC))
            cp = pr.op("vector", f_cp, waits=[tr, st["xn_pe"]], inc="v")
            bank_free[pk] = cp
            xn_tile_ready[t] = cp
            if with_route:
                ph_route_logits_tile(t)
            warm(W_LN)
        if regb_stage:
            st["regB_pe"] = pr.mark("pe")

    def ph_route_logits_tile(t):
        bk = 4
        def f_mm(tn, t=t):
            ins = None
            for c in range(NC):
                ins = tn.matmul(ps[2][:, t * 8:(t + 1) * 8],
                                xn[:, c * S + t * 128: c * S + (t + 1) * 128],
                                rT[:, c * 8:(c + 1) * 8],
                                start=(t == 0 and c == 0),
                                stop=(t == NT - 1 and c == NC - 1),
                                skip_group_check=True)
            return ins
        wt = [xn_tile_ready[t], W_RT]
        if t == 0:
            wt += wb([bk], "f")
        mm = pr.op("tensor", f_mm, waits=wt, inc="pe")
        if t == NT - 1:
            bank_ready[bk] = mm

    def ph_route():
        bk = 4
        mm = bank_ready[bk]
        st["xn_pe"] = pr.mark("pe")
        cp = pr.op("vector", lambda v: v.tensor_copy(Lg[:], ps[2][:, 0:128]),
                   waits=[mm], inc="v")
        bank_free[bk] = cp

        def c_(a, b):
            return rsc[:, a:b]
        for t in range(NT):
            L = Lg[:, t * 8:(t + 1) * 8]
            pr.op("vector", (lambda v, L=L: v.reduce_max(zsb[:, 32:33], L, axis=AX.X)),
                  waits=[cp])
            pr.op("vector", (lambda v, L=L: v.tensor_scalar(
                c_(32, 40), L, zsb[:, 32:33], None, ALU.subtract)))
            pr.op("vector", (lambda v, L=L: v.tensor_scalar(
                c_(40, 48), L, -1.0, None, ALU.mult)))
            pr.op("vector", (lambda v, L=L: v.tensor_tensor(
                c_(48, 56), L, c_(40, 48), ALU.max)))
            pr.op("vector", lambda v: v.tensor_scalar(
                c_(56, 64), c_(48, 56), zsb[:, 32:33], -0.02, ALU.max, ALU.mult))
            pr.op("vector", lambda v: v.tensor_tensor(
                c_(72, 80), c_(32, 40), c_(56, 64), ALU.is_ge))
            v1 = pr.op("vector", (lambda v, L=L: v.tensor_scalar(
                c_(80, 88), L, zsb[:, 32:33], None, ALU.is_equal)), inc="v")
            se = pr.op("scalar", lambda s: s.activation(
                c_(120, 128), c_(32, 40), AF.Exp), waits=[v1], inc="sc")
            pr.op("vector", lambda v: v.tensor_tensor(
                c_(128, 136), c_(120, 128), c_(72, 80), ALU.mult), waits=[se])
            pr.op("vector", lambda v: v.reduce_sum(zsb[:, 33:34], c_(128, 136), axis=AX.X))
            pr.op("vector", lambda v: v.reciprocal(zsb[:, 34:35], zsb[:, 33:34]))
            pr.op("vector", lambda v: v.tensor_scalar(
                zsb[:, 35:36], zsb[:, 34:35], 0.5, None, ALU.mult))
            pr.op("vector", (lambda v, L=L: v.scalar_tensor_tensor(
                c_(88, 96), c_(80, 88), -1.0e30, L, ALU.mult, ALU.add)))
            pr.op("vector", lambda v: v.reduce_max(zsb[:, 36:37], c_(88, 96), axis=AX.X))
            pr.op("vector", lambda v: v.tensor_scalar(
                c_(96, 104), c_(88, 96), zsb[:, 36:37], None, ALU.subtract))
            pr.op("vector", (lambda v, L=L: v.tensor_scalar(
                c_(104, 112), L, zsb[:, 36:37], None, ALU.subtract)))
            pr.op("vector", lambda v: v.tensor_scalar(
                c_(56, 64), c_(48, 56), zsb[:, 36:37], -0.02, ALU.max, ALU.mult))
            pr.op("vector", lambda v: v.tensor_tensor(
                c_(72, 80), c_(104, 112), c_(56, 64), ALU.is_ge))
            v2 = pr.op("vector", lambda v: v.tensor_scalar(
                c_(112, 120), c_(88, 96), zsb[:, 36:37], None, ALU.is_equal), inc="v")
            se2 = pr.op("scalar", lambda s: s.activation(
                c_(120, 128), c_(96, 104), AF.Exp), waits=[v2], inc="sc")
            pr.op("vector", lambda v: v.tensor_tensor(
                c_(128, 136), c_(120, 128), c_(72, 80), ALU.mult), waits=[se2])
            pr.op("vector", lambda v: v.reduce_sum(zsb[:, 37:38], c_(128, 136), axis=AX.X))
            pr.op("vector", lambda v: v.reciprocal(zsb[:, 38:39], zsb[:, 37:38]))
            pr.op("vector", lambda v: v.tensor_scalar(
                zsb[:, 39:40], zsb[:, 38:39], 0.5, None, ALU.mult))
            pr.op("vector", lambda v: v.tensor_scalar(
                c_(136, 144), c_(112, 120), zsb[:, 39:40], None, ALU.mult))
            pr.op("vector", (lambda v, t=t: v.scalar_tensor_tensor(
                dwh[:, t * 8:(t + 1) * 8], c_(80, 88), zsb[:, 35:36],
                c_(136, 144), ALU.mult, ALU.add)), inc="v")

    def ph_ffn():
        dwh_done = pr.mark("v")
        warm(W_PRE)
        wd_ready = None
        for e in range(NE):
            for th in range(2):
                grp_g = [None] * 4
                grp_u = [None] * 4

                def issue_grp(g, e=e):
                    for kind, buf, dram, tgt in (("g", wg_buf, wgT_d, grp_g),
                                                 ("u", wu_buf, wuT_d, grp_u)):
                        par = g % 2
                        wt = [wbuf_free.get((kind, par))]
                        def f_dma(gp, e=e, g=g, buf=buf, dram=dram, par=par):
                            return gp.dma_start(
                                buf[par][:].rearrange("p (c f) -> p c f", c=NC),
                                dram[e, :, g * 512:(g + 1) * 512]
                                .rearrange("(c p) f -> p c f", p=128))
                        tgt[g] = pr.op("gpsimd", f_dma, waits=wt, inc="w", n=16)

                issue_grp(0)
                issue_grp(1)
                if th == 0:
                    wt = [st["ln_dump"], st["wd_lastrd"]]
                    if e == 0:
                        wt.append(st["regA_pe"])
                    for half in range(2):
                        def f_wd(gp, e=e, half=half):
                            return gp.dma_start(
                                wd_sb[:, half * 8192:(half + 1) * 8192]
                                .rearrange("p (c h) -> p c h", c=8),
                                wdT_d[e, half * 1024:(half + 1) * 1024, :]
                                .rearrange("(c p) h -> p c h", p=128))
                        wd_ready = pr.op("gpsimd", f_wd, waits=wt, inc="w", n=16)

                hm_done = [None] * NFT
                for ft in range(NFT):
                    if ft == 4:
                        issue_grp(2)
                    if ft == 8:
                        issue_grp(3)
                    g = ft // 4
                    par = g % 2
                    gpi = ft % 2
                    upi = 2 + (ft % 2)
                    fo = (ft % 4) * 128
                    wt = [grp_g[g], xn_tile_ready[th * 8 + 7]] + wb(pair(gpi), "f")
                    def f_gate(tn, gpi=gpi, par=par, fo=fo, th=th):
                        ins = None
                        for c in range(NC):
                            lhsT = wg_buf[par][:, c * 512 + fo: c * 512 + fo + 128]
                            for n in range(2):
                                ins = tn.matmul(
                                    ps[gpi][:, n * 512:(n + 1) * 512], lhsT,
                                    xn[:, c * S + th * 1024 + n * 512:
                                       c * S + th * 1024 + (n + 1) * 512],
                                    start=(c == 0), stop=(c == NC - 1))
                        return ins
                    mmg = pr.op("tensor", f_gate, waits=wt, inc="pe")
                    for b in pair(gpi):
                        bank_ready[b] = mmg
                    wt2 = [grp_u[g]] + wb(pair(upi), "f")
                    def f_up(tn, upi=upi, par=par, fo=fo, th=th):
                        ins = None
                        for c in range(NC):
                            lhsT = wu_buf[par][:, c * 512 + fo: c * 512 + fo + 128]
                            for n in range(2):
                                ins = tn.matmul(
                                    ps[upi][:, n * 512:(n + 1) * 512], lhsT,
                                    xn[:, c * S + th * 1024 + n * 512:
                                       c * S + th * 1024 + (n + 1) * 512],
                                    start=(c == 0), stop=(c == NC - 1))
                        return ins
                    mmu = pr.op("tensor", f_up, waits=wt2, inc="pe")
                    for b in pair(upi):
                        bank_ready[b] = mmu
                    if ft % 4 == 3:
                        done = pr.mark("pe")
                        wbuf_free[("g", par)] = done
                        wbuf_free[("u", par)] = done
                    sl = pr.op("scalar", (lambda s, gpi=gpi, ft=ft: s.activation(
                        silu_sb[:, (ft % 2) * 1024:(ft % 2 + 1) * 1024],
                        ps[gpi][:, 0:1024], AF.Silu)),
                        waits=[mmg, st["silu_free"][ft % 2]], inc="sc")
                    for b in pair(gpi):
                        bank_free[b] = sl
                    hm = pr.op("vector", (lambda v, upi=upi, ft=ft: v.tensor_tensor(
                        h_sb[:, ft * 1024:(ft + 1) * 1024],
                        silu_sb[:, (ft % 2) * 1024:(ft % 2 + 1) * 1024],
                        ps[upi][:, 0:1024], ALU.mult)), waits=[mmu, sl], inc="v")
                    for b in pair(upi):
                        bank_free[b] = hm
                    st["silu_free"][ft % 2] = hm
                    hm_done[ft] = hm

                for tt in range(8):
                    pi = tt % 4
                    gt = th * 8 + tt
                    wt = wb(pair(pi), "f") + [hm_done[NFT - 1], wd_ready]
                    def f_down(tn, pi=pi, tt=tt):
                        ins = None
                        for fc in range(16):
                            lhsT = h_sb[:, fc * 1024 + tt * 128: fc * 1024 + (tt + 1) * 128]
                            for n in range(2):
                                ins = tn.matmul(
                                    ps[pi][:, n * 512:(n + 1) * 512], lhsT,
                                    wd_sb[:, fc * 1024 + n * 512: fc * 1024 + (n + 1) * 512],
                                    start=(fc == 0), stop=(fc == 15))
                        return ins
                    mmd = pr.op("tensor", f_down, waits=wt, inc="pe")
                    for b in pair(pi):
                        bank_ready[b] = mmd
                    ev = pr.op("vector", (lambda v, pi=pi, gt=gt, e=e: v.scalar_tensor_tensor(
                        x_sb[:, gt * D:(gt + 1) * D], ps[pi][:, 0:1024],
                        dwh[:, gt * 8 + e: gt * 8 + e + 1],
                        x_sb[:, gt * D:(gt + 1) * D], ALU.mult, ALU.add)),
                        waits=[mmd, dwh_done], inc="v")
                    st["xtw"][gt] = ev
                    for b in pair(pi):
                        bank_free[b] = ev
                st["wd_lastrd"] = pr.mark("pe")
        st["hm_last"] = pr.mark("v")
        st["regA_pe"] = pr.mark("pe")
        st["regB_pe"] = pr.mark("pe")
        st["regB_sc"] = pr.mark("sc")
        st["x_w"] = pr.mark("v")
        st["xn_pe"] = pr.mark("pe")
        for k in list(wbuf_free):
            wbuf_free[k] = None

    def ph_attn():
        zsbB = zsb.bitcast(BF16)
        aw_ready = {}
        for name, dst, srcd in (("v", wv_sb, wvT_d), ("o", wo_sb, woT_d),
                                ("q", wq_sb, wqT_d), ("k", wk_sb, wkT_d)):
            def f_dma(gp, dst=dst, srcd=srcd):
                return gp.dma_start(dst[:].rearrange("p (c f) -> p c f", c=NC),
                                    srcd[:, :].rearrange("(c p) f -> p c f", p=128))
            aw_ready[name] = pr.op(
                "gpsimd", f_dma,
                waits=[st["regA_pe"], st["wd_lastrd"]],
                inc="w", n=16)
        def f_bvt(tn):
            ins = None
            for c in range(NC):
                ins = tn.transpose(ps[2][0:1, c * 128:(c + 1) * 128],
                                   bpack[:, 16 + c:17 + c], identf[:])
            return ins
        bt = pr.op("tensor", f_bvt, waits=[LD_ALL] + wb([4, 5], "f"), inc="pe")
        brow = pr.op("vector", lambda v: v.tensor_copy(
            qf[0:1, 0:1024], ps[2][0:1, 0:1024]),
            waits=[bt, st["regB_pe"], st["regB_sc"]], inc="v")
        bank_free[4] = bank_free[5] = brow
        def f_bvb(tn):
            tn.matmul(ps[3][:, 0:512], onerow[:, :], qf[0:1, 0:512],
                      start=True, stop=True, skip_group_check=True)
            return tn.matmul(ps[3][:, 512:1024], onerow[:, :], qf[0:1, 512:1024],
                             start=True, stop=True, skip_group_check=True)
        bb = pr.op("tensor", f_bvb, waits=[brow] + wb([6, 7], "f"), inc="pe")
        bvb_done = pr.op("vector", lambda v: v.tensor_copy(bvb[:], ps[3][:, 0:1024]),
                         waits=[bb, st["regB_pe"]], inc="v")
        bank_free[6] = bank_free[7] = bvb_done
        sxl = None
        for c in range(NC):
            sxl = pr.op("vector", (lambda v, c=c: v.reduce_sum(
                zsb[:, 40 + c:41 + c], xn[:, c * S:(c + 1) * S], axis=AX.X)),
                waits=[xn_tile_ready[NT - 1]], inc="v")
        sxbf = pr.op("vector", lambda v: v.tensor_copy(
            zsbB[:, 0:8], zsb[:, 40:48]), inc="v")
        def f_sv(tn):
            ins = None
            for m in range(8):
                for c in range(NC):
                    ins = tn.matmul(ps[3][:, m:m + 1],
                                    wv_sb[:, c * 1024 + m * 128: c * 1024 + (m + 1) * 128],
                                    zsbB[:, c:c + 1],
                                    start=(c == 0), stop=(c == NC - 1),
                                    skip_group_check=True)
            return ins
        msv = pr.op("tensor", f_sv,
                    waits=[sxbf, bvb_done, aw_ready["v"]], inc="pe")
        sv_ev = pr.op("vector", lambda v: v.scalar_tensor_tensor(
            zsb[:, 32:40], bpack[:, 16:24], float(S), ps[3][:, 0:8],
            ALU.mult, ALU.add), waits=[msv], inc="v")
        svbf = pr.op("vector", lambda v: v.tensor_scalar(
            zsbB[:, 8:16], zsb[:, 32:40], 1.0e-9, None, ALU.mult), inc="v")
        def f_bot(tn):
            ins = None
            for c in range(NC):
                ins = tn.transpose(ps[2][0:1, c * 128:(c + 1) * 128],
                                   bpack[:, 24 + c:25 + c], identf[:])
            return ins
        mbo = pr.op("tensor", f_bot, waits=[brow], inc="pe")
        boev = pr.op("vector", lambda v: v.tensor_copy(
            qf[0:1, 2048:3072], ps[2][0:1, 0:1024]), waits=[mbo], inc="v")
        bank_free[4] = bank_free[5] = boev
        def f_corr(tn):
            ins = None
            for c in range(NC):
                for n in range(2):
                    ins = tn.matmul(ps[3][0:1, n * 512:(n + 1) * 512],
                                    zsbB[:, 8 + c:9 + c],
                                    wo_sb[:, c * 1024 + n * 512: c * 1024 + (n + 1) * 512],
                                    start=(c == 0), stop=(c == NC - 1),
                                    skip_group_check=True)
            return ins
        mcor = pr.op("tensor", f_corr,
                     waits=[svbf, sv_ev, aw_ready["o"]], inc="pe")
        cbrow = pr.op("vector", lambda v: v.tensor_tensor(
            qf[0:1, 3072:4096], ps[3][0:1, 0:1024], qf[0:1, 2048:3072],
            ALU.add), waits=[mcor, boev], inc="v")
        def f_cb(tn):
            tn.matmul(ps[3][:, 0:512], onerow[:, :], qf[0:1, 3072:3584],
                      start=True, stop=True, skip_group_check=True)
            return tn.matmul(ps[3][:, 512:1024], onerow[:, :], qf[0:1, 3584:4096],
                             start=True, stop=True, skip_group_check=True)
        mcb = pr.op("tensor", f_cb, waits=[cbrow], inc="pe")
        xadd = None
        for t in range(NT):
            xadd = pr.op("vector", (lambda v, t=t: v.tensor_tensor(
                x_sb[:, t * D:(t + 1) * D], ps[3][:, 0:1024],
                x_sb[:, t * D:(t + 1) * D], ALU.add)),
                waits=[mcb, st["xtw"][t]], inc="v")
            st["xtw"][t] = xadd
        bank_free[6] = bank_free[7] = xadd
        warm(W_PREATT)

        kslot_ready = [None] * 5
        vslot_ready = [None] * 5
        kslot_lastrd = [None] * 5
        vslot_lastrd = [None] * 5
        qf_lastrd = None
        e_free = None
        e_done = {"last": None}
        pend = {}

        def back_half(tn_state):
            T = tn_state["T"]
            pvl = tn_state["pv_last"]
            rz = tn_state["rz"]
            cv = None
            for h in range(NH):
                cv = pr.op("vector", (lambda v, h=h: v.tensor_scalar(
                    ctm[:, h * 64:(h + 1) * 64], ps[2][:, h * 64:(h + 1) * 64],
                    zsb[:, 16 + h:17 + h], None, ALU.mult)),
                    waits=[pvl, rz, st["ln_dump"]], inc="v")
            bank_free[4] = bank_free[5] = cv
            def f_ctr(tn):
                ins = None
                for c in range(NC):
                    ins = tn.transpose(psb[3][:, c * 128:(c + 1) * 128],
                                       ctm[:, c * 128:(c + 1) * 128], identb[:])
                return ins
            ctr = pr.op("tensor", f_ctr, waits=wb([6], "f") + [cv], inc="pe")
            bank_ready[6] = ctr
            cfm_cp = pr.op("vector", lambda v: v.tensor_copy(
                cfm[:], psb[3][:, 0:1024]), waits=[ctr], inc="v")
            bank_free[6] = cfm_cp
            def f_o(tn):
                ins = None
                for c in range(NC):
                    lhsT = cfm[:, c * 128:(c + 1) * 128]
                    for n in range(2):
                        ins = tn.matmul(ps[1][:, n * 512:(n + 1) * 512], lhsT,
                                        wo_sb[:, c * 1024 + n * 512: c * 1024 + (n + 1) * 512],
                                        start=(c == 0), stop=(c == NC - 1))
                return ins
            mo = pr.op("tensor", f_o,
                       waits=wb([2, 3], "f") + [cfm_cp, aw_ready["o"]], inc="pe")
            bank_ready[2] = bank_ready[3] = mo
            oev = pr.op("vector", (lambda v, T=T: v.tensor_tensor(
                x_sb[:, T * D:(T + 1) * D], ps[1][:, 0:1024],
                x_sb[:, T * D:(T + 1) * D], ALU.add)), waits=[mo], inc="v")
            st["xtw"][T] = oev
            bank_free[2] = bank_free[3] = oev

        for ch in range(4):
            tiles = list(range(4 * ch, 4 * ch + 4))
            tok0 = 4 * ch * 128
            if pend:
                back_half(pend)
                pend = {}
            for m in range(8):
                bk = m % 2
                wt = [aw_ready["q"], xn_tile_ready[tiles[-1]], qf_lastrd] + wb([bk], "f")
                def f_q(tn, m=m, bk=bk, tok0=tok0):
                    ins = None
                    for c in range(NC):
                        ins = tn.matmul(ps[0][:, bk * 512:(bk + 1) * 512],
                                        wq_sb[:, c * 1024 + m * 128: c * 1024 + (m + 1) * 128],
                                        xn[:, c * S + tok0: c * S + tok0 + 512],
                                        start=(c == 0), stop=(c == NC - 1))
                    return ins
                mq = pr.op("tensor", f_q, waits=wt, inc="pe")
                bank_ready[bk] = mq
                evq = pr.op("vector", (lambda v, m=m, bk=bk: v.tensor_scalar(
                    qf[:, m * 512:(m + 1) * 512], ps[0][:, bk * 512:(bk + 1) * 512],
                    bpack[:, m:m + 1], None, ALU.add)), waits=[mq], inc="v")
                bank_free[bk] = evq
            qf_ready = pr.mark("v")
            for m in range(8):
                bk = 2 + m % 2
                wt = [aw_ready["k"]] + wb([bk], "f")
                if m < 2:
                    wt += [kslot_lastrd[T % 5] for T in tiles]
                def f_k(tn, m=m, bk=bk, tok0=tok0):
                    ins = None
                    for c in range(NC):
                        ins = tn.matmul(ps[1][:, (bk - 2) * 512:(bk - 1) * 512],
                                        wk_sb[:, c * 1024 + m * 128: c * 1024 + (m + 1) * 128],
                                        xn[:, c * S + tok0: c * S + tok0 + 512],
                                        start=(c == 0), stop=(c == NC - 1))
                    return ins
                mk = pr.op("tensor", f_k, waits=wt, inc="pe")
                bank_ready[bk] = mk
                evk = None
                for i, T in enumerate(tiles):
                    sl = T % 5
                    evk = pr.op("vector", (lambda v, m=m, bk=bk, i=i, sl=sl: v.tensor_scalar(
                        kf[:, sl * 1024 + m * 128: sl * 1024 + (m + 1) * 128],
                        ps[1][:, (bk - 2) * 512 + i * 128:(bk - 2) * 512 + (i + 1) * 128],
                        bpack[:, 8 + m:9 + m], None, ALU.add)), waits=[mk], inc="v")
                bank_free[bk] = evk
            for T in tiles:
                kslot_ready[T % 5] = pr.mark("v")
            for i, T in enumerate(tiles):
                sl = T % 5
                wt = [aw_ready["v"]] + wb([4, 5], "f")
                def f_v(tn, i=i, tok0=tok0):
                    ins = None
                    for c in range(NC):
                        lhsT = xn[:, c * S + tok0 + i * 128: c * S + tok0 + (i + 1) * 128]
                        for n in range(2):
                            ins = tn.matmul(ps[2][:, n * 512:(n + 1) * 512], lhsT,
                                            wv_sb[:, c * 1024 + n * 512: c * 1024 + (n + 1) * 512],
                                            start=(c == 0), stop=(c == NC - 1))
                    return ins
                mv = pr.op("tensor", f_v, waits=wt, inc="pe")
                bank_ready[4] = bank_ready[5] = mv
                evv = pr.op("vector", (lambda v, sl=sl: v.tensor_tensor(
                    vt[:, sl * 1024:(sl + 1) * 1024], ps[2][:, 0:1024], bvb[:],
                    ALU.add)), waits=[mv, bvb_done, vslot_lastrd[sl]], inc="v")
                bank_free[4] = bank_free[5] = evv
                vslot_ready[sl] = evv
            for T in tiles:
                iic = T - 4 * ch
                epar = (T % 2) * 1024
                jts = ([(0, (T - 1) % 5)] if T > 0 else []) + [(1, T % 5)]
                z_open = False
                pv_open = {4: False, 5: False}
                zmm_last = None
                pv_last = None
                for hg in range(4):
                    wt = wb([0, 1], "f") + \
                        [kslot_ready[sl] for (_, sl) in jts] + [qf_ready]
                    def f_sc(tn, hg=hg, jts=jts, iic=iic):
                        ins = None
                        for hh in range(4):
                            h = 4 * hg + hh
                            po = (hh % 2) * 64
                            for (jt, sl) in jts:
                                col = (hh % 2) * 512 + (hh // 2) * 256 + jt * 128
                                ins = tn.matmul(
                                    ps[0][:, col:col + 128],
                                    kf[po:po + 64, sl * 1024 + (h // 2) * 128:
                                       sl * 1024 + (h // 2 + 1) * 128],
                                    qf[po:po + 64, (h // 2) * 512 + iic * 128:
                                       (h // 2) * 512 + (iic + 1) * 128],
                                    start=True, stop=True,
                                    tile_position=(po, 0), skip_group_check=True)
                        return ins
                    msc = pr.op("tensor", f_sc, waits=wt, inc="pe")
                    bank_ready[0] = bank_ready[1] = msc
                    warm(W_ATT)
                    exl = None
                    for par in range(2):
                        if T > 0:
                            ma = pr.op("vector", (lambda v, par=par: v.tensor_tensor(
                                stmp[:], ps[0][:, par * 512:(par + 1) * 512],
                                maskwA[:], ALU.add)),
                                waits=[msc, LD_ALL, e_free, e_done.get("last")], inc="v")
                            ex = pr.op("scalar", (lambda s, par=par, epar=epar: s.activation(
                                E_sb[:, epar + par * 512: epar + (par + 1) * 512],
                                stmp[:], AF.Exp, scale=SCALE)),
                                waits=[ma, e_free, st["regB_pe"]], inc="sc")
                        else:
                            ma = pr.op("vector", (lambda v, par=par: v.tensor_tensor(
                                stmp[:, 0:256].rearrange("p (a f) -> p a f", a=2),
                                bass.AP(ps[0], par * 512 + 128,
                                        [[1024, 128], [256, 2], [1, 128]]),
                                maskwB[:].rearrange("p (a f) -> p a f", a=2), ALU.add)),
                                waits=[msc, LD_ALL, e_free, e_done.get("last")], inc="v")
                            ex = pr.op("scalar", (lambda s, par=par, epar=epar: s.activation(
                                bass.AP(E_sb, epar + par * 512 + 128,
                                        [[2048, 128], [256, 2], [1, 128]]),
                                stmp[:, 0:256].rearrange("p (a f) -> p a f", a=2),
                                AF.Exp, scale=SCALE)),
                                waits=[ma, e_free, st["regB_pe"]], inc="sc")
                        e_done["last"] = ex
                        exl = ex
                    bank_free[0] = bank_free[1] = e_done["last"]
                    for (jt, _) in jts:
                        e_done[jt] = exl
                    zplan = []
                    pvplan = []
                    for hh in range(4):
                        h = 4 * hg + hh
                        bnk = 4 + (h // 8)
                        ecol = epar + (hh % 2) * 512 + (hh // 2) * 256
                        for k, (jt, sl) in enumerate(jts):
                            zst = not z_open
                            z_open = True
                            zsp = (hg == 3 and hh == 3 and k == len(jts) - 1)
                            zplan.append((h, ecol + jt * 128, zst, zsp))
                            pst = not pv_open[bnk]
                            pv_open[bnk] = True
                            psp = (hh == 3 and k == len(jts) - 1 and hg % 2 == 1)
                            pvplan.append((h, ecol + jt * 128, sl, pst, psp))
                    def f_z(tn, zplan=tuple(zplan)):
                        ins = None
                        for (h, ec, zst, zsp) in zplan:
                            ins = tn.matmul(
                                ps[3][:, 512 + h:513 + h],
                                E_sb[:, ec:ec + 128],
                                onecol[:], start=zst, stop=zsp,
                                skip_group_check=True)
                        return ins
                    zwt = [e_done[jt] for (jt, _) in jts]
                    if hg == 0:
                        zwt += wb([7], "f")
                    zmm_last = pr.op("tensor", f_z, waits=zwt, inc="pe")
                    if PIPE_ATTN and hg == 0 and pend:
                        back_half(pend)
                        pend = {}
                    def f_pv(tn, pvplan=tuple(pvplan)):
                        ins = None
                        for (h, ec, sl, pst, psp) in pvplan:
                            ins = tn.matmul(
                                ps[2][:, h * 64:(h + 1) * 64],
                                E_sb[:, ec:ec + 128],
                                vt[:, sl * 1024 + h * 64: sl * 1024 + (h + 1) * 64],
                                start=pst, stop=psp, skip_group_check=True)
                        return ins
                    pvwt = [e_done[jt] for (jt, _) in jts] + \
                           [vslot_ready[sl] for (_, sl) in jts]
                    if hg == 0:
                        pvwt += wb([4, 5], "f")
                    pv_last = pr.op("tensor", f_pv, waits=pvwt, inc="pe")
                    e_free = pr.mark("pe")
                bank_ready[7] = zmm_last
                bank_ready[4] = bank_ready[5] = pv_last
                for (jt, sl) in jts:
                    kslot_lastrd[sl] = pr.mark("pe")
                    vslot_lastrd[sl] = pr.mark("pe")
                qf_lastrd = pr.mark("pe")
                zev = pr.op("vector", lambda v: v.tensor_copy(
                    zsb[:, 0:16], ps[3][:, 512:528]), waits=[zmm_last], inc="v")
                bank_free[7] = zev
                rz = pr.op("vector", lambda v: v.reciprocal(
                    zsb[:, 16:32], zsb[:, 0:16]), inc="v")
                pend = {"T": T, "pv_last": pv_last, "rz": rz}
                if not PIPE_ATTN:
                    back_half(pend)
                    pend = {}
        if pend:
            back_half(pend)
        st["hm_last"] = pr.mark("pe")
        st["regA_pe"] = pr.mark("pe")
        st["regB_pe"] = pr.mark("pe")
        st["regB_sc"] = pr.mark("sc")
        st["x_w"] = pr.mark("v")
        st["xn_pe"] = pr.mark("pe")
        st["wd_lastrd"] = pr.mark("pe")
        for k in ("g", "u"):
            for p_ in range(2):
                wbuf_free[(k, p_)] = st["regB_pe"]

    def ph_store():
        def f_st(sy):
            return sy.dma_start(
                out_d[:, :].rearrange("(t p) h -> p t h", p=128),
                x_sb[:].rearrange("p (t h) -> p t h", t=NT))
        pr.op("sync", f_st, waits=[st["x_w"]], inc="st", n=16)
        if debug_dump:
            dxn = nc.dram_tensor("dbg_xn", [128, NC * S], BF16, kind="ExternalOutput")
            dh = nc.dram_tensor("dbg_h", [128, NFT * 1024], BF16, kind="ExternalOutput")
            ddw = nc.dram_tensor("dbg_dwh", [128, 128], F32, kind="ExternalOutput")
            dlg = nc.dram_tensor("dbg_lg", [128, 128], F32, kind="ExternalOutput")
            dst_ = nc.dram_tensor("dbg_stats", [128, 128], F32, kind="ExternalOutput")
            wfin = [st["x_w"], pr.mark("pe"), pr.mark("v"), pr.mark("sc")]
            pr.op("sync", lambda sy: sy.dma_start(dxn[:, :], xn[:]),
                  waits=wfin, inc="st", n=16)
            pr.op("sync", lambda sy: sy.dma_start(dh[:, :], h_sb[:]), inc="st", n=16)
            pr.op("sync", lambda sy: sy.dma_start(ddw[:, :], dwh[:]), inc="st", n=16)
            pr.op("sync", lambda sy: sy.dma_start(dlg[:, :], Lg[:]), inc="st", n=16)
            pr.op("sync", lambda sy: sy.dma_start(dst_[:, :], stats[:]), inc="st", n=16)
            drs = nc.dram_tensor("dbg_rsc", [128, 160], F32, kind="ExternalOutput")
            dzs = nc.dram_tensor("dbg_zsb", [128, 48], F32, kind="ExternalOutput")
            pr.op("sync", lambda sy: sy.dma_start(drs[:, :], rsc[:]), inc="st", n=16)
            pr.op("sync", lambda sy: sy.dma_start(dzs[:, :], zsb[:]), inc="st", n=16)

    ph_load()
    ph_ln(with_route=True, first=True)
    ph_route()
    if "ffn1" in stages:
        ph_ffn()
    if "attn" in stages:
        ph_ln(regb_stage=True)
        ph_attn()
    if "ffn2" in stages:
        ph_ln(with_route=True, regb_stage=True)
        ph_route()
        ph_ffn()
    ph_store()

    with (
        nc.semaphore("s_ld") as s_ld,
        nc.semaphore("s_w") as s_w,
        nc.semaphore("s_pe") as s_pe,
        nc.semaphore("s_v") as s_v,
        nc.semaphore("s_sc") as s_sc,
        nc.semaphore("s_st") as s_st,
        nc.semaphore("s_vv") as s_vv,
        nc.Block() as block,
    ):
        sems = {"ld": s_ld, "w": s_w, "pe": s_pe, "v": s_v, "sc": s_sc, "st": s_st}

        def replay(name, tail=None):
            serial = name == "vector"

            def go(eng):
                lastw = {}
                cnt2 = {}
                prev = None
                for fn, waits, inc, n, nospace in pr.streams[name]:
                    if serial and prev is not None and not nospace:
                        eng.wait_ge(prev[0], prev[1])
                    for (s, val) in waits:
                        if lastw.get(s, 0) < val:
                            eng.wait_ge(sems[s], val)
                            lastw[s] = val
                    ins = fn(eng)
                    if inc is not None:
                        assert ins is not None, name
                        ins.then_inc(sems[inc], n)
                        cnt2[inc] = cnt2.get(inc, 0) + n
                        if serial:
                            prev = (sems[inc], cnt2[inc])
                    elif serial:
                        assert ins is not None, f"{name} op returned None"
                        ins.then_inc(s_vv, 1)
                        cnt2["vv"] = cnt2.get("vv", 0) + 1
                        prev = (s_vv, cnt2["vv"])
                if tail is not None:
                    tail(eng)
            return go

        block.tensor(replay("tensor"))
        block.vector(replay("vector"))
        block.scalar(replay("scalar"))
        block.gpsimd(replay("gpsimd"))
        block.sync(replay("sync", tail=lambda sy: sy.wait_ge(s_st, pr.cnt["st"])))

    return nc


_CACHE = {}


def _host_consts():
    ident_b = np.eye(128, dtype=ml_dtypes.bfloat16)
    ident_f = np.eye(128, dtype=np.float32)
    jj = np.arange(128)[:, None]
    ii = np.arange(128)[None, :]
    m1 = np.where(jj <= ii, 0.0, NEGBIG).astype(ml_dtypes.bfloat16)
    m0 = np.where(jj >= ii, 0.0, NEGBIG).astype(ml_dtypes.bfloat16)
    maskwA = np.ascontiguousarray(np.tile(np.concatenate([m0, m1], axis=1), (1, 2)))
    maskwB = np.ascontiguousarray(np.tile(m1, (1, 2)))
    onecol = np.ones((128, 1), dtype=ml_dtypes.bfloat16)
    onerow = np.ones((1, 128), dtype=ml_dtypes.bfloat16)
    return ident_b, ident_f, maskwA, maskwB, onecol, onerow


def kernel(L0_attn_x, L0_attn_Wq, L0_attn_bq, L0_attn_Wk, L0_attn_bk,
           L0_attn_Wv, L0_attn_bv, L0_attn_Wo, L0_attn_bo,
           L0_ffn_router_w, L0_ffn_w_gate, L0_ffn_w_up, L0_ffn_w_down,
           _stages=ALL_STAGES):
    x = np.ascontiguousarray(np.asarray(L0_attn_x, dtype=np.float32))
    B = x.shape[0]
    wgT = np.ascontiguousarray(np.asarray(L0_ffn_w_gate).transpose(0, 2, 1))
    wuT = np.ascontiguousarray(np.asarray(L0_ffn_w_up).transpose(0, 2, 1))
    wdT = np.ascontiguousarray(np.asarray(L0_ffn_w_down).transpose(0, 2, 1))
    wqT = np.ascontiguousarray(np.asarray(L0_attn_Wq).T)
    wkT = np.ascontiguousarray(np.asarray(L0_attn_Wk).T)
    wvT = np.ascontiguousarray(np.asarray(L0_attn_Wv).T)
    woT = np.ascontiguousarray(np.asarray(L0_attn_Wo).T)
    rTp = np.ascontiguousarray(
        np.asarray(L0_ffn_router_w).reshape(NE, NC, 128).transpose(2, 1, 0)
        .reshape(128, NC * NE))
    bpack = np.ascontiguousarray(np.concatenate(
        [np.asarray(b, dtype=np.float32).reshape(8, 128).T
         for b in (L0_attn_bq, L0_attn_bk, L0_attn_bv, L0_attn_bo)], axis=1))
    ident_b, ident_f, maskwA, maskwB, onecol, onerow = _host_consts()

    key = ("prog", tuple(_stages))
    if key not in _CACHE:
        _CACHE[key] = build_program(_stages)
    nc = _CACHE[key]

    shared = {"wgT": wgT, "wuT": wuT, "wdT": wdT, "wqT": wqT, "wkT": wkT,
              "wvT": wvT, "woT": woT, "rT": rTp, "bpack": bpack,
              "identb": ident_b, "identf": ident_f,
              "maskwA": maskwA, "maskwB": maskwB,
              "onecol": onecol, "onerow": onerow}
    in_maps = [dict(shared, x=np.ascontiguousarray(x[c])) for c in range(B)]
    res = run_bass_kernel_spmd(nc, in_maps, core_ids=list(range(B)))
    out = np.stack([np.asarray(res.results[c]["out"]) for c in range(B)], axis=0)
    return out.astype(np.float32)
